# version 01
# baseline (speedup 1.0000x reference)
import numpy as np
from contextlib import ExitStack
import concourse.bass as bass
import concourse.mybir as mybir
from concourse.bass_utils import run_bass_kernel_spmd

F32 = mybir.dt.float32
BF16 = mybir.dt.bfloat16
F16 = mybir.dt.float16
I32 = mybir.dt.int32
U8 = mybir.dt.uint8
AF = mybir.ActivationFunctionType
ALU = mybir.AluOpType
AX = mybir.AxisListType

T = 2048
S = 2
DM = 2048
H = 16
QC = 512
RD = 64
FFI = 5632
GU = 2 * FFI
EPS_LN = 1e-5
EPS_A = 1e-6
EPS_B = 1e-5
SCA = 1.0 / float(np.sqrt(128.0))

N_PHASES = 4
DBG = 0
DBG_G = 0

ENGS = ("t", "v", "s", "g", "y")


class KB:
    def __init__(self):
        self.ops = {e: [] for e in ENGS}
        self.cnt = {e: 0 for e in ENGS}
        self.lastw = {e: {e2: 0 for e2 in ENGS} for e in ENGS}

    def op(self, e, fn, wait=(), inc=False):
        wait = list(wait)
        if e in ("v", "s"):
            inc = True
            if self.cnt[e] > 0:
                wait.append((e, self.cnt[e]))
        for tok in wait:
            if tok is None:
                continue
            se, v = tok
            if self.lastw[e][se] >= v:
                continue
            self.lastw[e][se] = v
            self.ops[e].append(("w", se, v))
        tok = None
        if inc:
            self.cnt[e] += 16 if e in ("g", "y") else 1
            tok = (e, self.cnt[e])
        self.ops[e].append(("i", fn, tok))
        return tok

    def tok(self, e):
        return (e, self.cnt[e])

    def barrier(self):
        snap = dict(self.cnt)
        for e in ENGS:
            for e2 in ENGS:
                if e2 == e or snap[e2] == 0 or self.lastw[e][e2] >= snap[e2]:
                    continue
                self.lastw[e][e2] = snap[e2]
                self.ops[e].append(("w", e2, snap[e2]))


class PB:
    def __init__(self, ap):
        self.ap = ap
        self.w = None
        self.r = None


def build_nc():
    nc = bass.Bass("TRN2", target_bir_lowering=False)
    d = {}
    def din(name, shape, dt):
        d[name] = nc.dram_tensor(name, shape, dt, kind="ExternalInput").ap()
    din("x", [S * T, DM], F32)
    din("wqc", [DM, QC], F32)
    din("wqu", [QC, DM], F32)
    din("wk", [DM, 128], F32)
    din("wv", [DM, 128], F32)
    din("w1", [DM, DM], F32)
    din("w2", [4 * DM, DM], F32)
    din("wqb", [DM, DM], F32)
    din("wkb", [DM, DM], F32)
    din("wvb", [DM, DM], F32)
    din("wob", [DM, DM], F32)
    din("gup_p", [GU, DM // 8], I32)
    din("gup_s", [GU, 16], F16)
    din("dwn_p", [DM, FFI // 8], I32)
    din("dwn_s", [DM, 44], F16)
    din("ident", [128, 128], F32)
    din("cos_t", [T, 32], F32)
    din("sin_t", [T, 32], F32)
    din("m_su", [128, 128], F32)
    din("m_li", [128, 128], F32)
    din("m_ui", [128, 128], F32)
    din("qnw4", [128, 512], F32)
    din("knw", [128, 128], F32)
    din("gam4", [128, 512], F32)
    din("bet4", [128, 512], F32)
    din("sinkb", [128, 16], F32)
    din("cvec", [128, 8], F32)
    y = nc.dram_tensor("y", [S * T, DM], F32, kind="ExternalOutput").ap()
    d_dbg = nc.dram_tensor("dbg", [128, 45056], BF16, kind="ExternalOutput").ap()
    d_dbg2 = nc.dram_tensor("dbg2", [128, 64], F32, kind="ExternalOutput").ap()
    d_dbg3 = nc.dram_tensor("dbg3", [128, 16384], BF16, kind="ExternalOutput").ap()

    def scr(name, shape, dt):
        return nc.dram_tensor(name, shape, dt, kind="Internal").ap()
    b_wqc = scr("b_wqc", [DM, QC], BF16)
    b_wqu = scr("b_wqu", [QC, DM], BF16)
    b_wkv = scr("b_wkv", [DM, 256], BF16)
    b_wc = scr("b_wc", [DM, DM], BF16)
    b_w1t = scr("b_w1t", [4 * DM, 512], BF16)
    b_qT = scr("b_qT", [DM, DM], BF16)
    b_kT = scr("b_kT", [DM, DM], BF16)
    b_vT = scr("b_vT", [DM, DM], BF16)
    b_oT = scr("b_oT", [DM, DM], BF16)
    b_gup = scr("b_gup", [DM, GU], BF16)
    b_dwn = scr("b_dwn", [FFI, DM], BF16)
    kb_d = scr("kb_d", [S * H * 128, T], BF16)
    vb_d = scr("vb_d", [S * T, DM], BF16)
    b_ysc = scr("b_ysc", [1024, DM], F32)

    kb = KB()
    st = ExitStack()
    def sb(name, shape, dt):
        return st.enter_context(nc.sbuf_tensor(name, shape, dt))

    AF32 = sb("AF32", [128, 10240], F32)
    ABA = sb("ABA", [128, 45056], BF16)
    WS = sb("WS", [128, 16384], BF16)
    SM = sb("SM", [128, 8192], BF16)
    CL = sb("CL", [128, 64], F32)
    IP = sb("IP", [128, 1440], I32)
    c_id = sb("c_id", [128, 128], F32)
    c_idb = sb("c_idb", [128, 128], BF16)
    c_cos = sb("c_cos", [128, 512], F32)
    c_sin = sb("c_sin", [128, 512], F32)
    c_msu = sb("c_msu", [128, 128], BF16)
    c_mli = sb("c_mli", [128, 128], BF16)
    c_mui = sb("c_mui", [128, 128], BF16)
    c_qnw = sb("c_qnw", [128, 512], F32)
    c_knw = sb("c_knw", [128, 128], F32)
    c_gam = sb("c_gam", [128, 512], F32)
    c_bet = sb("c_bet", [128, 512], F32)
    c_sink = sb("c_sink", [128, 16], F32)
    c_cv = sb("c_cv", [128, 8], F32)
    c_ones = sb("c_ones", [128, 1], BF16)
    c_oner = sb("c_oner", [1, 128], BF16)
    c_mf = sb("c_mf", [128, 384], F32)

    p = [st.enter_context(nc.psum_tensor(f"p{i}", [128, 512], F32)) for i in range(6)]
    pt = [st.enter_context(nc.psum_tensor(f"pt{i}", [128, 1024], BF16)) for i in range(2)]
    P = [PB(x_[:]) for x_ in p]
    PT = [PB(x_[:, 0:128]) for x_ in pt]

    def mm(bank, lhsT, rhs, start, stop, wait=(), oap=None):
        w = list(wait)
        if start and bank.r is not None:
            w.append(bank.r)
        out = bank.ap if oap is None else oap
        tok = kb.op("t", lambda e, o=out, l=lhsT, r=rhs, s0=start, s1=stop:
                    e.matmul(o, l, r, start=s0, stop=s1), wait=w, inc=stop)
        if stop:
            bank.w = tok
        return tok

    def tr(bank, in_, wait=()):
        w = list(wait)
        if bank.r is not None:
            w.append(bank.r)
        tok = kb.op("t", lambda e, o=bank.ap, i=in_: e.transpose(o, i, c_idb[:]),
                    wait=w, inc=True)
        bank.w = tok
        return tok

    def evac(eng, bank, out, fn=None, wait=(), src=None):
        w = list(wait)
        w.append(bank.w)
        i_ap = bank.ap if src is None else src
        if fn is None:
            if eng == "v":
                tok = kb.op(eng, lambda e, o=out, i=i_ap: e.tensor_copy(o, i), wait=w, inc=True)
            else:
                tok = kb.op(eng, lambda e, o=out, i=i_ap: e.copy(o, i), wait=w, inc=True)
        else:
            tok = kb.op(eng, lambda e, o=out, i=i_ap, f=fn: f(e, o, i), wait=w, inc=True)
        bank.r = tok
        return tok

    def dma(q, out, in_, wait=()):
        o, i = out, in_
        if len(i.shape) == 3 and len(o.shape) == 2:
            o = o.rearrange("q (c n) -> q c n", n=i.shape[2])
        elif len(o.shape) == 3 and len(i.shape) == 2:
            i = i.rearrange("q (c n) -> q c n", n=o.shape[2])
        return kb.op(q, lambda e, o=o, i=i: e.dma_start(o, i), wait=wait, inc=True)

    tok_c = []
    tok_c.append(dma("y", c_id[:], d["ident"]))
    tok_c.append(dma("y", c_cos[:], d["cos_t"].rearrange("(a q) c -> q a c", q=128)))
    tok_c.append(dma("y", c_sin[:], d["sin_t"].rearrange("(a q) c -> q a c", q=128)))
    tok_c.append(dma("y", c_mf[:, 0:128], d["m_su"]))
    tok_c.append(dma("y", c_mf[:, 128:256], d["m_li"]))
    tok_c.append(dma("y", c_mf[:, 256:384], d["m_ui"]))
    tok_c.append(dma("y", c_qnw[:], d["qnw4"]))
    tok_c.append(dma("y", c_knw[:], d["knw"]))
    tok_c.append(dma("y", c_gam[:], d["gam4"]))
    tok_c.append(dma("y", c_bet[:], d["bet4"]))
    tok_c.append(dma("y", c_sink[:], d["sinkb"]))
    tok_c.append(dma("y", c_cv[:], d["cvec"]))
    kb.op("v", lambda e: e.tensor_copy(c_idb[:], c_id[:]), wait=tok_c, inc=False)
    kb.op("v", lambda e: e.tensor_copy(c_msu[:], c_mf[:, 0:128]), inc=False)
    kb.op("v", lambda e: e.tensor_copy(c_mli[:], c_mf[:, 128:256]), inc=False)
    kb.op("v", lambda e: e.tensor_copy(c_mui[:], c_mf[:, 256:384]), inc=False)
    kb.op("v", lambda e: e.memset(c_ones[:], 1.0), inc=False)
    kb.op("v", lambda e: e.memset(c_oner[:], 1.0), inc=True)
    kb.barrier()

    B_M046 = c_cv[:, 0:1]
    B_75 = c_cv[:, 1:2]
    B_M8 = c_cv[:, 2:3]
    B_M20 = c_cv[:, 3:4]

    def cast_simple(src, dst, nchunk, ncols):
        for c0 in range(0, nchunk, 4):
            cc = min(4, nchunk - c0)
            ld = dma("y", AF32[:, 0:cc * ncols],
                     src[c0 * 128:(c0 + cc) * 128, :].rearrange("(c q) n -> q c n", q=128))
            cv = kb.op("v", lambda e, n=cc * ncols: e.tensor_copy(SM[:, 0:n], AF32[:, 0:n]),
                       wait=[ld], inc=True)
            dma("g", dst[c0 * 128:(c0 + cc) * 128, :].rearrange("(c q) n -> q c n", q=128),
                SM[:, 0:cc * ncols], wait=[cv])
            kb.barrier()

    cast_simple(d["wqc"], b_wqc, 16, 512)
    cast_simple(d["wqu"], b_wqu, 4, 2048)
    for c0 in range(0, 16, 4):
        ldk = dma("y", AF32[:, 0:512],
                  d["wk"][c0 * 128:(c0 + 4) * 128, :].rearrange("(c q) n -> q c n", q=128))
        ldv = dma("y", AF32[:, 512:1024],
                  d["wv"][c0 * 128:(c0 + 4) * 128, :].rearrange("(c q) n -> q c n", q=128))
        cv = kb.op("v", lambda e: e.tensor_copy(SM[:, 0:1024], AF32[:, 0:1024]),
                   wait=[ldk, ldv], inc=True)
        kv2 = None
        for j in range(4):
            kb.op("v", lambda e, j=j: e.tensor_copy(SM[:, 1024 + j * 256:1024 + j * 256 + 128],
                                                    SM[:, j * 128:(j + 1) * 128]), inc=False)
            kv2 = kb.op("v", lambda e, j=j: e.tensor_copy(SM[:, 1152 + j * 256:1152 + j * 256 + 128],
                                                          SM[:, 512 + j * 128:512 + (j + 1) * 128]), inc=True)
        dma("g", b_wkv[c0 * 128:(c0 + 4) * 128, :].rearrange("(c q) n -> q c n", q=128),
            SM[:, 1024:2048], wait=[kv2])
        kb.barrier()

    def transpose_w(src, dst):
        for j in range(16):
            ld = dma("y", AF32[:, 0:2048],
                     src[:, j * 128:(j + 1) * 128].rearrange("(c q) n -> q c n", q=128))
            cv = kb.op("v", lambda e: e.tensor_copy(SM[:, 0:2048], AF32[:, 0:2048]),
                       wait=[ld], inc=True)
            last = None
            for c in range(16):
                b = PT[c % 2]
                tr(b, SM[:, c * 128:(c + 1) * 128], wait=[cv])
                last = evac("v" if c % 2 == 0 else "s", b, SM[:, 2048 + c * 128:2048 + (c + 1) * 128])
            dma("g", dst[j * 128:(j + 1) * 128, :], SM[:, 2048:4096],
                wait=[last, kb.tok("v"), kb.tok("s")])
            kb.barrier()

    transpose_w(d["wqb"], b_qT)
    transpose_w(d["wkb"], b_kT)
    transpose_w(d["wvb"], b_vT)
    transpose_w(d["wob"], b_oT)

    for g in range(4):
        ld = dma("y", AF32[:, 0:8192],
                 d["w1"][g * 512:(g + 1) * 512, :].rearrange("(c q) n -> q c n", q=128))
        cv = kb.op("v", lambda e: e.tensor_copy(SM[:, 0:8192], AF32[:, 0:8192]),
                   wait=[ld], inc=True)
        for jc in range(16):
            last = None
            for ic in range(4):
                b = PT[ic % 2]
                tr(b, SM[:, ic * 2048 + jc * 128: ic * 2048 + (jc + 1) * 128], wait=[cv])
                last = evac("v" if ic % 2 == 0 else "s", b, WS[:, ic * 128:(ic + 1) * 128])
            dma("g", b_w1t[g * 2048 + jc * 128: g * 2048 + (jc + 1) * 128, :],
                WS[:, 0:512], wait=[last, kb.tok("v"), kb.tok("s")])
            kb.barrier()

    for g in range(4):
        lds = []
        for j in range(16):
            lds.append(dma("y", WS[:, j * 512:(j + 1) * 512],
                           b_w1t[g * 2048 + j * 128: g * 2048 + (j + 1) * 128, :]))
        wc_slot = [None, None]
        prev_cvw_last = None
        for dsl in range(4):
            cvw = []
            for j in range(16):
                ldw = dma("y", AF32[:, (j % 2) * 512:(j % 2) * 512 + 512],
                          d["w2"][g * 2048 + j * 128: g * 2048 + (j + 1) * 128,
                                  dsl * 512:(dsl + 1) * 512],
                          wait=[cvw[j - 2]] if j >= 2 else [prev_cvw_last])
                cvw.append(kb.op("v", lambda e, j=j: e.tensor_copy(
                    WS[:, 8192 + j * 512:8192 + (j + 1) * 512],
                    AF32[:, (j % 2) * 512:(j % 2) * 512 + 512]),
                    wait=[ldw] + ([kb.tok("t")] if j == 0 else []), inc=True))
            prev_cvw_last = cvw[-1]
            for ic in range(4):
                bank = P[ic % 2]
                for j in range(16):
                    mm(bank, WS[:, j * 512 + ic * 128: j * 512 + ic * 128 + 128],
                       WS[:, 8192 + j * 512:8192 + (j + 1) * 512],
                       start=(j == 0), stop=(j == 15),
                       wait=(lds + [cvw[j]]) if ic == 0 else [cvw[j]])
                ev = evac("v", bank, SM[:, (ic % 2) * 512:(ic % 2) * 512 + 512],
                          wait=[wc_slot[ic % 2]])
                wc_slot[ic % 2] = dma("g", b_wc[(g * 4 + ic) * 128:(g * 4 + ic + 1) * 128,
                                                dsl * 512:(dsl + 1) * 512],
                                      SM[:, (ic % 2) * 512:(ic % 2) * 512 + 512], wait=[ev])
        kb.barrier()

    def dequant(packed, scales_t, n_oc, in_cols, n_grp, dst):
        u8_cols = in_cols // 2
        sub = 1024 if u8_cols % 1024 == 0 else u8_cols // 2
        nsub = u8_cols // sub
        sc16 = SM[:, 4096:4160].bitcast(F16)
        u8buf = SM[:, 0:u8_cols // 2].bitcast(U8)
        prev_dma = None
        wrow_guard = None
        for oc in range(n_oc):
            lp = dma("g", u8buf, packed[oc * 128:(oc + 1) * 128, :].bitcast(U8),
                     wait=[prev_dma, kb.tok("t")])
            ls = dma("y", sc16[:, 0:n_grp], scales_t[oc * 128:(oc + 1) * 128, :],
                     wait=[prev_dma])
            kb.op("v", lambda e, n=n_grp: e.tensor_copy(CL[:, 0:n], sc16[:, 0:n]),
                  wait=[ls], inc=False)
            sc2 = kb.op("v", lambda e, n=n_grp: e.tensor_scalar_mul(CL[:, 0:n], CL[:, 0:n], 0.5),
                        inc=True)
            wr_toks = []
            for si in range(nsub):
                s0 = si * sub
                n = sub
                bf = AF32[:, 0:n]
                tf = AF32[:, 1440:1440 + n]
                hi = AF32[:, 2880:2880 + n]
                lo = AF32[:, 4320:4320 + n]
                sg = AF32[:, 5760:5760 + n]
                t1 = AF32[:, 7200:7200 + n]
                r1 = AF32[:, 8640:8640 + n]
                bft = kb.op("v", lambda e, o=bf, i=u8buf[:, s0:s0 + n]: e.tensor_copy(o, i),
                            wait=[lp], inc=True)
                kb.op("v", lambda e, o=tf, i=bf: e.tensor_scalar(o, i, 0.0625, -0.46875,
                                                                 ALU.mult, ALU.add), inc=False)
                kb.op("v", lambda e, o=IP[:, 0:n], i=tf: e.tensor_copy(o, i), inc=False)
                kb.op("v", lambda e, o=hi, i=IP[:, 0:n]: e.tensor_copy(o, i), inc=False)
                lot = kb.op("v", lambda e, o=lo, i=hi, b2=bf: e.scalar_tensor_tensor(
                    o, i, -16.0, b2, ALU.mult, ALU.add), inc=True)
                vt = None
                for pi, pl in enumerate((lo, hi)):
                    sgt = kb.op("s", lambda e, o=sg, i=pl: e.activation(o, i, AF.Sign, bias=B_75, scale=-1.0),
                                wait=[lot, vt], inc=True)
                    t1t = kb.op("v", lambda e, o=t1, i=sg, b2=pl: e.scalar_tensor_tensor(
                        o, i, 4.0, b2, ALU.mult, ALU.add), wait=[sgt], inc=True)
                    kb.op("s", lambda e, o=r1, i=t1: e.activation(o, i, AF.Relu, bias=B_M8),
                          wait=[t1t], inc=False)
                    r2t = kb.op("s", lambda e, o=bf, i=t1: e.activation(o, i, AF.Relu, bias=B_M20, scale=2.0),
                                inc=True)
                    kb.op("v", lambda e, o=r1, a=r1, b2=bf: e.tensor_add(o, a, b2),
                          wait=[r2t], inc=False)
                    kb.op("v", lambda e, o=t1, i=t1, b2=r1: e.scalar_tensor_tensor(
                        o, i, -4.0, b2, ALU.add, ALU.add), inc=False)
                    vt = kb.op("v", lambda e, o=t1, a=t1, b2=sg: e.tensor_mul(o, a, b2), inc=True)
                    ncg = n // 64
                    for gg2 in range(ncg):
                        gabs = (s0 + gg2 * 64) // 64
                        wview = WS[:, 0:2 * u8_cols].rearrange("q (a b) -> q a b", b=2)[
                            :, s0 + gg2 * 64: s0 + (gg2 + 1) * 64, pi]
                        tokw = kb.op("v", lambda e, o=wview, i=t1[:, gg2 * 64:(gg2 + 1) * 64],
                                     s_=CL[:, gabs:gabs + 1]: e.tensor_scalar(o, i, s_, None, ALU.mult),
                                     wait=[sc2, wrow_guard], inc=(gg2 == ncg - 1))
                    wr_toks.append(tokw)
            nblk = in_cols // 128
            last = None
            for c in range(nblk):
                b = PT[c % 2]
                tr(b, WS[:, c * 128:(c + 1) * 128], wait=wr_toks if c == 0 else [])
                last = evac("v" if c % 2 == 0 else "s", b,
                            WS[:, 8192 + c * 128: 8192 + (c + 1) * 128],
                            wait=[prev_dma] if c < 2 else [])
            wrow_guard = kb.tok("t")
            prev_dma = dma("g", dst.rearrange("(c q) o -> q c o", q=128)[:, :, oc * 128:(oc + 1) * 128],
                           WS[:, 8192:8192 + nblk * 128].rearrange("q (c n) -> q c n", n=128),
                           wait=[last, kb.tok("v"), kb.tok("s")])
        kb.barrier()

    if DBG == 6:
        t1 = dma("y", SM[:, 0:512], b_w1t[0:128, :])
        t2 = dma("y", SM[:, 512:1024], b_w1t[2000 * 4:2000 * 4 + 0, :]) if False else dma("y", SM[:, 512:1024], b_w1t[2048 + 256:2048 + 384, :])
        t3 = dma("y", SM[:, 1024:1536], b_w1t[8064:8192, :])
        dma("g", d_dbg3[:, 0:1536], SM[:, 0:1536], wait=[t1, t2, t3])
        kb.barrier()
    if DBG == 5:
        t1 = dma("y", SM[:, 0:2048], b_wc[0:128, :])
        t2 = dma("y", SM[:, 2048:4096], b_wc[1920:2048, :])
        t3 = dma("y", SM[:, 4096:6144], b_wc[640:768, :])
        dma("g", d_dbg3[:, 0:6144], SM[:, 0:6144], wait=[t1, t2, t3])
        kb.barrier()
    dequant(d["gup_p"], d["gup_s"], GU // 128, DM, 16, b_gup)
    dequant(d["dwn_p"], d["dwn_s"], DM // 128, FFI, 44, b_dwn)
    if DBG == 2:
        t1 = dma("y", SM[:, 0:2048], b_gup[:, 0:128].rearrange("(c q) n -> q c n", q=128))
        t2 = dma("y", SM[:, 2048:4096], b_gup[:, 11136:11264].rearrange("(c q) n -> q c n", q=128))
        t3 = dma("y", SM[:, 4096:6144], b_dwn[0:128, :])
        t4 = dma("y", SM[:, 6144:8192], b_dwn[5504:5632, :])
        dma("g", d_dbg3[:, 0:8192], SM[:, 0:8192], wait=[t1, t2, t3, t4])
        kb.barrier()

    def ln_tile(xap, out_bf, cb, wait=(), sq_wait=(), out_wait=()):
        kb.op("v", lambda e, i=xap: e.reduce_sum(CL[:, cb:cb + 1], i, axis=AX.X), wait=wait, inc=True)
        sq = kb.op("s", lambda e, i=xap: e.activation(SM[:, 6144:8192], i, AF.Square,
                                                      accum_out=CL[:, cb + 1:cb + 2]),
                   wait=list(wait) + list(sq_wait), inc=True)
        kb.op("v", lambda e: e.tensor_scalar_mul(CL[:, cb:cb + 1], CL[:, cb:cb + 1], 1.0 / DM),
              wait=[sq], inc=False)
        kb.op("v", lambda e: e.tensor_mul(CL[:, cb + 2:cb + 3], CL[:, cb:cb + 1], CL[:, cb:cb + 1]),
              inc=False)
        vv = kb.op("v", lambda e: e.scalar_tensor_tensor(CL[:, cb + 3:cb + 4], CL[:, cb + 1:cb + 2],
                                                         1.0 / DM, CL[:, cb + 2:cb + 3],
                                                         ALU.mult, ALU.subtract), inc=True)
        sq2 = kb.op("s", lambda e: e.activation(CL[:, cb + 3:cb + 4], CL[:, cb + 3:cb + 4], AF.Sqrt,
                                                bias=c_cv[:, 5:6]), wait=[vv], inc=True)
        kb.op("v", lambda e: e.reciprocal(CL[:, cb + 4:cb + 5], CL[:, cb + 3:cb + 4]),
              wait=[sq2], inc=False)
        return kb.op("v", lambda e, i=xap, o=out_bf: e.tensor_scalar(
            o, i, CL[:, cb:cb + 1], CL[:, cb + 4:cb + 5], ALU.subtract, ALU.mult),
            wait=out_wait, inc=True)

    if N_PHASES >= 1:
        LNT, QFM, OFM, KA, VA, LTM, Q1, PRB, PBT, QTM = (
            0, 8192, 16384, 24576, 28672, 32768, 34816, 36864, 37504, 38144)
        for s_ in range(S):
            for g in range(4):
                kb.barrier()
                wl1 = dma("y", WS[:, 0:8192], b_wqc.rearrange("(c q) n -> q c n", q=128))
                wl2 = dma("y", WS[:, 8192:12288], b_wqu[0:256, :].rearrange("(c q) n -> q c n", q=128))
                wl3 = dma("y", WS[:, 12288:16384], b_wqu[256:512, :].rearrange("(c q) n -> q c n", q=128))
                wl4 = dma("g", SM[:, 0:4096], b_wkv.rearrange("(c q) n -> q c n", q=128))
                xg = AF32[:, 0:8192]
                ldx = dma("y", xg, d["x"][(s_ * T + g * 512):(s_ * T + (g + 1) * 512), :]
                          .rearrange("(a q) n -> q a n", q=128))
                ev_lnT = []
                prev_lt = None
                for t_ in range(4):
                    lt = ln_tile(xg[:, t_ * 2048:(t_ + 1) * 2048],
                                 ABA[:, LTM:LTM + 2048], (t_ % 2) * 8, wait=[ldx],
                                 sq_wait=[prev_lt], out_wait=[kb.tok("t")])
                    prev_lt = lt
                    for c in range(16):
                        b = PT[c % 2]
                        tr(b, ABA[:, LTM + c * 128:LTM + (c + 1) * 128], wait=[lt])
                        ev_lnT.append(evac("v" if c % 2 == 0 else "s", b,
                                           ABA[:, LNT + c * 512 + t_ * 128:LNT + c * 512 + (t_ + 1) * 128]))
                ev_q1 = []
                for m in range(4):
                    bank = P[m % 2]
                    for k in range(16):
                        mm(bank, WS[:, k * 512 + m * 128: k * 512 + m * 128 + 128],
                           ABA[:, LNT + k * 512:LNT + (k + 1) * 512],
                           start=(k == 0), stop=(k == 15),
                           wait=([wl1] + ev_lnT) if k == 0 and m == 0 else [])
                    ev_q1.append(evac("v", bank, ABA[:, Q1 + m * 512:Q1 + (m + 1) * 512]))
                for t_ in range(4):
                    ti = g * 4 + t_
                    bank = P[2]
                    for k in range(16):
                        mm(bank, ABA[:, LNT + k * 512 + t_ * 128:LNT + k * 512 + (t_ + 1) * 128],
                           SM[:, k * 256:(k + 1) * 256],
                           start=(k == 0), stop=(k == 15),
                           wait=[wl4] if k == 0 else [], oap=p[2][:, 0:256])
                    evac("v", bank, ABA[:, VA + (s_ * 16 + ti) * 128:VA + (s_ * 16 + ti + 1) * 128],
                         src=p[2][:, 128:256])
                    qt = AF32[:, 8192:8320]
                    cosg = c_cos[:, ti * 32:(ti + 1) * 32]
                    sing = c_sin[:, ti * 32:(ti + 1) * 32]
                    kb.op("v", lambda e, o=qt[:, 64:128], i=p[2][:, 64:128]: e.tensor_copy(o, i),
                          wait=[bank.w], inc=False)
                    kb.op("v", lambda e, o=qt[:, 0:32], i=p[2][:, 0:32], c=cosg: e.tensor_mul(o, i, c), inc=False)
                    kb.op("v", lambda e, o=AF32[:, 8320:8352], i=p[2][:, 32:64], s2=sing: e.tensor_mul(o, i, s2), inc=False)
                    kb.op("v", lambda e, o=qt[:, 32:64], i=p[2][:, 0:32], s2=sing: e.tensor_mul(o, i, s2), inc=False)
                    rtok = kb.op("v", lambda e, o=AF32[:, 8352:8384], i=p[2][:, 32:64], c=cosg:
                                 e.tensor_mul(o, i, c), inc=True)
                    bank.r = rtok
                    kb.op("v", lambda e, o=qt[:, 0:32], a=qt[:, 0:32], b2=AF32[:, 8320:8352]:
                          e.tensor_sub(o, a, b2), inc=False)
                    kr = kb.op("v", lambda e, o=qt[:, 32:64], a=qt[:, 32:64], b2=AF32[:, 8352:8384]:
                               e.tensor_add(o, a, b2), inc=True)
                    sqk = kb.op("s", lambda e, o=SM[:, 4160:4288], i=qt: e.activation(
                        o[:, 0:128], i, AF.Square, accum_out=CL[:, 40:41]), wait=[kr], inc=True)
                    v1 = kb.op("v", lambda e: e.tensor_scalar(CL[:, 41:42], CL[:, 40:41], 1.0 / 128.0,
                                                              EPS_A, ALU.mult, ALU.add), wait=[sqk], inc=True)
                    s1 = kb.op("s", lambda e: e.activation(CL[:, 41:42], CL[:, 41:42], AF.Sqrt),
                               wait=[v1], inc=True)
                    kb.op("v", lambda e: e.reciprocal(CL[:, 42:43], CL[:, 41:42]), wait=[s1], inc=False)
                    kb.op("v", lambda e, i=qt: e.tensor_scalar_mul(i, i, CL[:, 42:43]), inc=False)
                    ktok = kb.op("v", lambda e, o=ABA[:, QTM:QTM + 128], i=qt, kn=c_knw[:]:
                                 e.tensor_mul(o, i, kn), wait=[kb.tok("t")], inc=True)
                    b = PT[t_ % 2]
                    tr(b, ABA[:, QTM:QTM + 128], wait=[ktok])
                    evac("v", b, ABA[:, KA + s_ * 2048 + ti * 128:KA + s_ * 2048 + (ti + 1) * 128])
                for t_ in range(4):
                    ti = g * 4 + t_
                    cosg = c_cos[:, ti * 32:(ti + 1) * 32]
                    sing = c_sin[:, ti * 32:(ti + 1) * 32]
                    CQ = AF32[:, 8448:8576]
                    SQ = AF32[:, 8576:8704]
                    for r in range(4):
                        kb.op("v", lambda e, o=CQ[:, r * 32:(r + 1) * 32], i=cosg: e.tensor_copy(o, i), inc=False)
                        kb.op("v", lambda e, o=SQ[:, r * 32:(r + 1) * 32], i=sing: e.tensor_copy(o, i),
                              inc=(r == 3))
                    cs_tok = kb.tok("v")
                    for hq in range(4):
                        bank = P[3 + (hq % 2)]
                        for m in range(4):
                            mm(bank, ABA[:, Q1 + m * 512 + t_ * 128:Q1 + m * 512 + (t_ + 1) * 128],
                               WS[:, 8192 + m * 2048 + hq * 512: 8192 + m * 2048 + (hq + 1) * 512],
                               start=(m == 0), stop=(m == 3),
                               wait=([wl2, wl3] + ev_q1) if hq == 0 and t_ == 0 else [])
                        pv = bank.ap
                        QT4 = AF32[:, 8704:9216]
                        RPA = AF32[:, 9216:9344]
                        RPB = AF32[:, 9344:9472]
                        def hv(ap, off, wdt):
                            return ap.rearrange("q (h c) -> q h c", c=128)[:, :, off:off + wdt]
                        def hv32(ap):
                            return ap.rearrange("q (h c) -> q h c", c=32)
                        kb.op("v", lambda e, o=hv(QT4, 64, 64), i=hv(pv, 64, 64): e.tensor_copy(o, i),
                              wait=[bank.w, cs_tok], inc=False)
                        kb.op("v", lambda e, o=hv(QT4, 0, 32), a=hv(pv, 0, 32), c=hv32(CQ):
                              e.tensor_tensor(o, a, c, ALU.mult), inc=False)
                        kb.op("v", lambda e, o=hv32(RPA), a=hv(pv, 32, 32), s2=hv32(SQ):
                              e.tensor_tensor(o, a, s2, ALU.mult), inc=False)
                        kb.op("v", lambda e, o=hv(QT4, 32, 32), a=hv(pv, 0, 32), s2=hv32(SQ):
                              e.tensor_tensor(o, a, s2, ALU.mult), inc=False)
                        rp2 = kb.op("v", lambda e, o=hv32(RPB), a=hv(pv, 32, 32), c=hv32(CQ):
                                    e.tensor_tensor(o, a, c, ALU.mult), inc=True)
                        bank.r = rp2
                        kb.op("v", lambda e, o=hv(QT4, 0, 32), a=hv(QT4, 0, 32), b2=hv32(RPA):
                              e.tensor_tensor(o, a, b2, ALU.subtract), inc=False)
                        rtok = kb.op("v", lambda e, o=hv(QT4, 32, 32), a=hv(QT4, 32, 32), b2=hv32(RPB):
                                     e.tensor_tensor(o, a, b2, ALU.add), inc=True)
                        for hh in range(4):
                            kb.op("s", lambda e, i=QT4[:, hh * 128:(hh + 1) * 128], hh=hh:
                                  e.activation(SM[:, 4160:4288], i, AF.Square, accum_out=CL[:, 44 + hh:45 + hh]),
                                  wait=[rtok], inc=(hh == 3))
                        sall = kb.tok("s")
                        v1 = kb.op("v", lambda e: e.tensor_scalar(CL[:, 48:52], CL[:, 44:48], 1.0 / 128.0,
                                                                  EPS_A, ALU.mult, ALU.add), wait=[sall], inc=True)
                        s1 = kb.op("s", lambda e: e.activation(CL[:, 48:52], CL[:, 48:52], AF.Sqrt),
                                   wait=[v1], inc=True)
                        kb.op("v", lambda e: e.reciprocal(CL[:, 52:56], CL[:, 48:52]), wait=[s1], inc=False)
                        for hh in range(4):
                            kb.op("v", lambda e, i=QT4[:, hh * 128:(hh + 1) * 128], hh=hh:
                                  e.tensor_scalar_mul(i, i, CL[:, 52 + hh:53 + hh]), inc=False)
                        qbf = SM[:, 4288:4800]
                        qbft = kb.op("v", lambda e, o=qbf, i=QT4, w=c_qnw[:]: e.tensor_mul(o, i, w),
                                     wait=[kb.tok("t")], inc=True)
                        for hh in range(4):
                            h = hq * 4 + hh
                            b = PT[hh % 2]
                            tr(b, qbf[:, hh * 128:(hh + 1) * 128], wait=[qbft])
                            evac("v" if hh % 2 == 0 else "s", b,
                                 ABA[:, QFM + h * 512 + t_ * 128:QFM + h * 512 + (t_ + 1) * 128])
                kb.barrier()
                if DBG == 1 and s_ == 0 and g == DBG_G:
                    dma("g", d_dbg, ABA[:])
                    dma("g", d_dbg2, CL[:])
                    kb.barrier()
                prb_guard = None
                for t_ in range(4):
                    ti = g * 4 + t_
                    b0 = max(0, ti - 4)
                    nb = ti - b0 + 1
                    wl = (nb - 1) * 128
                    for h in range(H):
                        qap = ABA[:, QFM + h * 512 + t_ * 128:QFM + h * 512 + (t_ + 1) * 128]
                        bl, bd, bo = P[4], P[5], P[3]
                        if wl > 0:
                            mm(bl, qap, ABA[:, KA + s_ * 2048 + b0 * 128:KA + s_ * 2048 + b0 * 128 + wl],
                               start=True, stop=True, oap=p[4][:, 0:wl])
                        mm(bd, qap, ABA[:, KA + s_ * 2048 + ti * 128:KA + s_ * 2048 + (ti + 1) * 128],
                           start=True, stop=True, oap=p[5][:, 0:128])
                        if wl > 0:
                            kb.op("v", lambda e, w=wl: e.reduce_max(CL[:, 56:57], p[4][:, 0:w], axis=AX.X),
                                  wait=[bl.w], inc=False)
                            kb.op("v", lambda e: e.reduce_max(CL[:, 57:58], p[5][:, 0:128], axis=AX.X),
                                  wait=[bd.w], inc=False)
                            kb.op("v", lambda e: e.tensor_tensor(CL[:, 58:59], CL[:, 56:57], CL[:, 57:58],
                                                                 ALU.max), inc=False)
                        else:
                            kb.op("v", lambda e: e.reduce_max(CL[:, 58:59], p[5][:, 0:128], axis=AX.X),
                                  wait=[bd.w], inc=False)
                        mtok = kb.op("v", lambda e: e.tensor_scalar_mul(CL[:, 59:60], CL[:, 58:59], -SCA),
                                     inc=True)
                        prb = ABA[:, PRB:PRB + 640]
                        mtoks = []
                        if wl > 0:
                            e1 = kb.op("s", lambda e, o=prb[:, 0:wl], w=wl: e.activation(
                                o, p[4][:, 0:w], AF.Exp, bias=CL[:, 59:60], scale=SCA),
                                wait=[mtok, prb_guard], inc=True)
                            bl.r = e1
                            mtoks.append(e1)
                        e2 = kb.op("s", lambda e, o=prb[:, wl:wl + 128]: e.activation(
                            o, p[5][:, 0:128], AF.Exp, bias=CL[:, 59:60], scale=SCA),
                            wait=[mtok, prb_guard], inc=True)
                        bd.r = e2
                        mtoks.append(e2)
                        if nb == 5:
                            mtoks.append(kb.op("v", lambda e, o=prb[:, 0:128], m=c_msu[:]:
                                               e.tensor_mul(o, o, m), wait=list(mtoks), inc=True))
                        mtoks.append(kb.op("v", lambda e, o=prb[:, wl:wl + 128], m=c_mli[:]:
                                           e.tensor_mul(o, o, m), wait=list(mtoks), inc=True))
                        kb.op("v", lambda e, w=wl: e.reduce_sum(CL[:, 60:61], prb[:, 0:w + 128], axis=AX.X),
                              wait=list(mtoks), inc=False)
                        sk = kb.op("s", lambda e, h=h: e.activation(CL[:, 61:62], CL[:, 58:59], AF.Exp,
                                                                    bias=c_sink[:, h:h + 1], scale=-SCA),
                                   wait=[mtok], inc=True)
                        kb.op("v", lambda e: e.tensor_add(CL[:, 60:61], CL[:, 60:61], CL[:, 61:62]),
                              wait=[sk], inc=False)
                        kb.op("v", lambda e: e.reciprocal(CL[:, 62:63], CL[:, 60:61]), inc=False)
                        ntok = kb.op("v", lambda e, w=wl: e.tensor_scalar_mul(
                            prb[:, 0:w + 128], prb[:, 0:w + 128], CL[:, 62:63]), inc=True)
                        for bi in range(nb):
                            bb = PT[bi % 2]
                            tr(bb, prb[:, bi * 128:(bi + 1) * 128], wait=[ntok])
                            ev = evac("v" if bi % 2 == 0 else "s", bb,
                                      ABA[:, PBT + bi * 128:PBT + (bi + 1) * 128],
                                      wait=[prb_guard])
                            mm(bo, ABA[:, VA + (s_ * 16 + b0 + bi) * 128:VA + (s_ * 16 + b0 + bi + 1) * 128],
                               ABA[:, PBT + bi * 128:PBT + (bi + 1) * 128],
                               start=(bi == 0), stop=(bi == nb - 1), wait=[ev], oap=p[3][:, 0:128])
                        evac("v", bo, ABA[:, OFM + h * 512 + t_ * 128:OFM + h * 512 + (t_ + 1) * 128],
                             src=p[3][:, 0:128])
                        prb_guard = kb.tok("t")
                kb.barrier()
                if DBG == 4 and s_ == 0 and g == DBG_G:
                    dma("g", d_dbg, ABA[:])
                    kb.barrier()
                for sl in range(4):
                    buf = (sl % 2) * 8192
                    wlc = dma("y", WS[:, buf:buf + 8192],
                              b_wc[:, sl * 512:(sl + 1) * 512].rearrange("(c q) n -> q c n", q=128),
                              wait=[kb.tok("t")] if sl >= 2 else [])
                    for t_ in range(4):
                        bank = P[t_ % 2]
                        for c in range(16):
                            mm(bank, ABA[:, OFM + c * 512 + t_ * 128:OFM + c * 512 + (t_ + 1) * 128],
                               WS[:, buf + c * 512:buf + (c + 1) * 512],
                               start=(c == 0), stop=(c == 15), wait=[wlc] if c == 0 else [])
                        evac("v", bank, xg[:, t_ * 2048 + sl * 512: t_ * 2048 + (sl + 1) * 512],
                             fn=lambda e, o, i: e.tensor_add(o, o, i))
                kb.barrier()
                dma("g", y[(s_ * T + g * 512):(s_ * T + (g + 1) * 512), :]
                    .rearrange("(a q) n -> q a n", q=128), xg)
        kb.barrier()

    def ffn_phase():
        for s_ in range(S):
            for gg in range(2):
                kb.barrier()
                row0 = s_ * T + gg * 1024
                prev_lt = None
                for t_ in range(8):
                    xb = AF32[:, (t_ % 2) * 2048:(t_ % 2) * 2048 + 2048]
                    ldx = dma("y", xb, y[row0 + t_ * 128: row0 + (t_ + 1) * 128, :],
                              wait=[kb.tok("v"), kb.tok("s")] if t_ >= 2 else [])
                    lt = ln_tile(xb, SM[:, 4096:6144], (t_ % 2) * 8, wait=[ldx],
                                 sq_wait=[prev_lt], out_wait=[kb.tok("t")])
                    prev_lt = lt
                    for c in range(16):
                        b = PT[c % 2]
                        tr(b, SM[:, 4096 + c * 128:4096 + (c + 1) * 128], wait=[lt])
                        evac("v" if c % 2 == 0 else "s", b,
                             WS[:, c * 1024 + t_ * 128: c * 1024 + (t_ + 1) * 128])
                kb.barrier()
                slot_tok = [None] * 4
                for j in range(44):
                    sg_ = 2 * (j % 2)
                    su_ = sg_ + 1
                    lg = dma("y", SM[:, sg_ * 2048:(sg_ + 1) * 2048],
                             b_gup[:, j * 128:(j + 1) * 128].rearrange("(c q) n -> q c n", q=128),
                             wait=[slot_tok[sg_]])
                    lu = dma("y", SM[:, su_ * 2048:(su_ + 1) * 2048],
                             b_gup[:, (44 + j) * 128:(45 + j) * 128].rearrange("(c q) n -> q c n", q=128),
                             wait=[slot_tok[su_]])
                    for tb in range(2):
                        bg = P[tb]
                        bu = P[2 + tb]
                        for k in range(16):
                            mm(bg, SM[:, sg_ * 2048 + k * 128: sg_ * 2048 + (k + 1) * 128],
                               WS[:, k * 1024 + tb * 512: k * 1024 + tb * 512 + 512],
                               start=(k == 0), stop=(k == 15), wait=[lg] if k == 0 else [])
                        for k in range(16):
                            mm(bu, SM[:, su_ * 2048 + k * 128: su_ * 2048 + (k + 1) * 128],
                               WS[:, k * 1024 + tb * 512: k * 1024 + tb * 512 + 512],
                               start=(k == 0), stop=(k == 15), wait=[lu] if k == 0 else [])
                        sil = kb.op("s", lambda e, o=AF32[:, 4096 + tb * 512:4608 + tb * 512], i=bg.ap:
                                    e.activation(o, i, AF.Silu), wait=[bg.w], inc=True)
                        bg.r = sil
                        hvt = kb.op("v", lambda e, o=ABA[:, j * 1024 + tb * 512: j * 1024 + tb * 512 + 512],
                                    a=AF32[:, 4096 + tb * 512:4608 + tb * 512], b2=bu.ap:
                                    e.tensor_tensor(o, a, b2, ALU.mult), wait=[sil, bu.w], inc=True)
                        bu.r = hvt
                        if tb == 1:
                            slot_tok[sg_] = hvt
                            slot_tok[su_] = hvt
                kb.barrier()
                if DBG == 3 and s_ == 0 and gg == 0:
                    dma("g", d_dbg, ABA[:])
                    dma("g", d_dbg3, WS[:])
                    kb.barrier()
                ysl_tok = [None, None]
                prev_mm = None
                for sl in range(8):
                    ch_tok = []
                    for k in range(44):
                        ch_tok.append(dma("y", WS[:, k * 256:(k + 1) * 256],
                                          b_dwn[k * 128:(k + 1) * 128, sl * 256:(sl + 1) * 256],
                                          wait=[prev_mm] if k == 0 else []))
                    for t_ in range(8):
                        bank = P[t_ % 2]
                        for k in range(44):
                            prev_mm = mm(bank, ABA[:, k * 1024 + t_ * 128:k * 1024 + (t_ + 1) * 128],
                                         WS[:, k * 256:(k + 1) * 256],
                                         start=(k == 0), stop=(k == 43),
                                         wait=[ch_tok[43]] if k == 0 else [],
                                         oap=p[t_ % 2][:, 0:256])
                        ysl = SM[:, (t_ % 2) * 512:(t_ % 2) * 512 + 512].bitcast(F32)[:, 0:256]
                        ev = evac("v", bank, ysl, src=p[t_ % 2][:, 0:256],
                                  wait=[ysl_tok[t_ % 2]])
                        ysl_tok[t_ % 2] = dma("g", b_ysc[t_ * 128:(t_ + 1) * 128, sl * 256:(sl + 1) * 256],
                                              ysl, wait=[ev])
                kb.barrier()
                if DBG == 7 and s_ == S - 1 and gg == 0:
                    t1 = dma("y", SM[:, 0:4096].bitcast(F32), b_ysc[0:128, :])
                    t2 = dma("y", SM[:, 4096:8192].bitcast(F32), b_ysc[512:640, :])
                    dma("g", d_dbg.bitcast(F32)[:, 0:4096], SM[:, 0:8192].bitcast(F32), wait=[t1, t2])
                    kb.barrier()
                buf_tok = [None, None]
                prev_adt = None
                for t_ in range(8):
                    cb = 16 + (t_ % 2) * 8
                    yb = AF32[:, (t_ % 2) * 2048:(t_ % 2) * 2048 + 2048]
                    xb = AF32[:, 4096 + (t_ % 2) * 2048:4096 + (t_ % 2) * 2048 + 2048]
                    ldy = dma("g", yb, b_ysc[t_ * 128:(t_ + 1) * 128, :], wait=[buf_tok[t_ % 2]])
                    ldx = dma("y", xb, y[row0 + t_ * 128: row0 + (t_ + 1) * 128, :],
                              wait=[buf_tok[t_ % 2]])
                    sq = kb.op("s", lambda e, i=yb, cb=cb: e.activation(SM[:, 6144:8192], i, AF.Square,
                                                                       accum_out=CL[:, cb:cb + 1]),
                               wait=[ldy, prev_adt], inc=True)
                    v1 = kb.op("v", lambda e, cb=cb: e.tensor_scalar(CL[:, cb + 1:cb + 2], CL[:, cb:cb + 1],
                                                                     1.0 / DM, EPS_A, ALU.mult, ALU.add),
                               wait=[sq], inc=True)
                    s1 = kb.op("s", lambda e, cb=cb: e.activation(CL[:, cb + 1:cb + 2], CL[:, cb + 1:cb + 2],
                                                                  AF.Sqrt), wait=[v1], inc=True)
                    kb.op("v", lambda e, cb=cb: e.reciprocal(CL[:, cb + 2:cb + 3], CL[:, cb + 1:cb + 2]),
                          wait=[s1], inc=False)
                    adt = kb.op("v", lambda e, o=xb, a=yb, b2=xb, cb=cb: e.scalar_tensor_tensor(
                        o, a, CL[:, cb + 2:cb + 3], b2, ALU.mult, ALU.add), wait=[ldx, ldy], inc=True)
                    prev_adt = adt
                    buf_tok[t_ % 2] = dma("g", y[row0 + t_ * 128: row0 + (t_ + 1) * 128, :], xb,
                                          wait=[adt])
        kb.barrier()

    if N_PHASES >= 2:
        ffn_phase()

    if N_PHASES >= 3:
        QFMB, LNT3, OFMB, KST, VST = 0, 16384, 16384, 32768, 40960
        for s_ in range(S):
            for gg in range(2):
                kb.barrier()
                row0 = s_ * T + gg * 1024
                pfx = gg * 1024 + 1024
                prev_lt = None
                for t_ in range(8):
                    xb = AF32[:, (t_ % 2) * 2048:(t_ % 2) * 2048 + 2048]
                    ldx = dma("y", xb, y[row0 + t_ * 128: row0 + (t_ + 1) * 128, :],
                              wait=[kb.tok("v"), kb.tok("s")] if t_ >= 2 else [])
                    lt = ln_tile(xb, SM[:, 4096:6144], (t_ % 2) * 8, wait=[ldx],
                                 sq_wait=[prev_lt], out_wait=[kb.tok("t")])
                    prev_lt = lt
                    for c in range(16):
                        b = PT[c % 2]
                        tr(b, SM[:, 4096 + c * 128:4096 + (c + 1) * 128], wait=[lt])
                        evac("v" if c % 2 == 0 else "s", b,
                             ABA[:, LNT3 + c * 1024 + t_ * 128: LNT3 + c * 1024 + (t_ + 1) * 128])
                kb.barrier()
                fr_tok = None
                kst_tok = {}
                vst_tok = {}
                for wt, kind in ((b_qT, "q"), (b_kT, "k"), (b_vT, "v")):
                    for hq in range(4):
                        buf = (hq % 2) * 8192
                        wld = dma("y", WS[:, buf:buf + 8192],
                                  wt[:, hq * 512:(hq + 1) * 512].rearrange("(c q) n -> q c n", q=128),
                                  wait=[kb.tok("t")])
                        for t_ in range(8):
                            bank = P[t_ % 2]
                            for k in range(16):
                                mm(bank, ABA[:, LNT3 + k * 1024 + t_ * 128:LNT3 + k * 1024 + (t_ + 1) * 128],
                                   WS[:, buf + k * 512:buf + (k + 1) * 512],
                                   start=(k == 0), stop=(k == 15), wait=[wld] if k == 0 else [])
                            if kind == "v":
                                ev = evac("v", bank,
                                          ABA[:, VST + (t_ % 2) * 2048 + hq * 512:
                                              VST + (t_ % 2) * 2048 + (hq + 1) * 512],
                                          wait=[vst_tok.get((t_ % 2, hq))])
                                vst_tok[(t_ % 2, hq)] = dma(
                                    "g", vb_d[row0 + t_ * 128: row0 + (t_ + 1) * 128,
                                              hq * 512:(hq + 1) * 512],
                                    ABA[:, VST + (t_ % 2) * 2048 + hq * 512:
                                        VST + (t_ % 2) * 2048 + (hq + 1) * 512], wait=[ev])
                                continue
                            fr = AF32[:, 4096:4608]
                            ev0 = evac("s", bank, fr, wait=[fr_tok])
                            kb.op("v", lambda e, o=fr, g2=c_gam[:]: e.tensor_mul(o, o, g2),
                                  wait=[ev0], inc=False)
                            t1tok = kb.op("v", lambda e, o=fr, b2=c_bet[:]: e.tensor_add(o, o, b2), inc=True)
                            for hh in range(4):
                                kb.op("v", lambda e, i=fr, hh=hh: e.reduce_sum(
                                    CL[:, 32 + hh:33 + hh], i[:, hh * 128:(hh + 1) * 128], axis=AX.X),
                                    wait=[t1tok], inc=(hh == 3))
                                kb.op("s", lambda e, i=fr, hh=hh: e.activation(
                                    SM[:, 4096:4224], i[:, hh * 128:(hh + 1) * 128], AF.Square,
                                    accum_out=CL[:, 36 + hh:37 + hh]),
                                    wait=[t1tok], inc=(hh == 3))
                            sall = kb.tok("s")
                            vall = kb.tok("v")
                            kb.op("v", lambda e: e.tensor_scalar_mul(CL[:, 32:36], CL[:, 32:36], 1.0 / 128.0),
                                  wait=[sall], inc=False)
                            kb.op("v", lambda e: e.tensor_mul(CL[:, 40:44], CL[:, 32:36], CL[:, 32:36]),
                                  inc=False)
                            kb.op("v", lambda e: e.tensor_scalar(CL[:, 36:40], CL[:, 36:40], 1.0 / 128.0,
                                                                 EPS_B, ALU.mult, ALU.add), inc=False)
                            vv = kb.op("v", lambda e: e.tensor_sub(CL[:, 36:40], CL[:, 36:40], CL[:, 40:44]),
                                       inc=True)
                            s1 = kb.op("s", lambda e: e.activation(CL[:, 36:40], CL[:, 36:40], AF.Sqrt),
                                       wait=[vv], inc=True)
                            kb.op("v", lambda e: e.reciprocal(CL[:, 40:44], CL[:, 36:40]), wait=[s1], inc=False)
                            if kind == "q":
                                kb.op("v", lambda e: e.tensor_scalar_mul(CL[:, 40:44], CL[:, 40:44],
                                                                         float(128.0 ** -0.5)), inc=False)
                            for hh in range(4):
                                h = hq * 4 + hh
                                w_extra = []
                                if kind == "k":
                                    w_extra = [kst_tok.get(h)]
                                nt = kb.op("v", lambda e, o=SM[:, 4224 + hh * 128:4224 + (hh + 1) * 128],
                                           i=fr, hh=hh: e.tensor_scalar(
                                               o, i[:, hh * 128:(hh + 1) * 128],
                                               CL[:, 32 + hh:33 + hh], CL[:, 40 + hh:41 + hh],
                                               ALU.subtract, ALU.mult),
                                           wait=[kb.tok("t")] if hh == 0 else [], inc=True)
                                fr_tok = nt
                                b = PT[hh % 2]
                                tr(b, SM[:, 4224 + hh * 128:4224 + (hh + 1) * 128], wait=[nt])
                                if kind == "q":
                                    evac("v" if hh % 2 == 0 else "s", b,
                                         ABA[:, QFMB + h * 1024 + t_ * 128:QFMB + h * 1024 + (t_ + 1) * 128])
                                else:
                                    evac("v" if hh % 2 == 0 else "s", b,
                                         ABA[:, KST + h * 512 + (t_ % 4) * 128:KST + h * 512 + (t_ % 4 + 1) * 128],
                                         wait=w_extra)
                            if kind == "k" and t_ % 4 == 3:
                                half = t_ // 4
                                for hh in range(4):
                                    h = hq * 4 + hh
                                    kst_tok[h] = dma(
                                        "g", kb_d[(s_ * H + h) * 128:(s_ * H + h + 1) * 128,
                                                  gg * 1024 + half * 512: gg * 1024 + (half + 1) * 512],
                                        ABA[:, KST + h * 512:KST + (h + 1) * 512],
                                        wait=[kb.tok("v"), kb.tok("s")])
                kb.barrier()
                nch = pfx // 128
                lk_tok = None
                lv_tok = None
                for h in range(H):
                    lk = dma("y", SM[:, 0:nch * 128],
                             kb_d[(s_ * H + h) * 128:(s_ * H + h + 1) * 128, 0:pfx],
                             wait=[lk_tok])
                    lv = dma("y", SM[:, 2048:2048 + nch * 128],
                             vb_d[s_ * T:s_ * T + pfx, h * 128:(h + 1) * 128]
                             .rearrange("(c q) n -> q c n", q=128), wait=[lv_tok])
                    ot_prev = None
                    att_guard = [None, None]
                    for tqb in range(2):
                        tq0 = gg * 1024 + tqb * 512
                        c0 = tq0 // 128
                        bo, bz = P[4], P[5]
                        ncc = c0 + 4
                        for c in range(ncc):
                            bsc = P[c % 2]
                            mm(bsc, SM[:, c * 128:(c + 1) * 128],
                               ABA[:, QFMB + h * 1024 + tqb * 512:QFMB + h * 1024 + (tqb + 1) * 512],
                               start=True, stop=True, wait=[lk])
                            att = SM[:, 4096 + (c % 2) * 512:4096 + (c % 2) * 512 + 512]
                            sq = kb.op("s", lambda e, o=att, i=bsc.ap: e.activation(o, i, AF.Square),
                                       wait=[bsc.w, ot_prev, att_guard[c % 2]], inc=True)
                            bsc.r = sq
                            mts = [sq]
                            if c >= c0:
                                f0 = (c - c0) * 128
                                if f0 > 0:
                                    mts.append(kb.op("v", lambda e, o=att, f=f0: e.memset(o[:, 0:f], 0.0),
                                                     wait=[sq], inc=True))
                                mts.append(kb.op("v", lambda e, o=att, f=f0, m=c_mui[:]:
                                                 e.tensor_mul(o[:, f:f + 128], o[:, f:f + 128], m),
                                                 wait=[sq], inc=True))
                            mm(bo, SM[:, 2048 + c * 128:2048 + (c + 1) * 128], att,
                               start=(c == 0), stop=(c == ncc - 1), wait=mts + [lv])
                            mm(bz, c_ones[:], att, start=(c == 0), stop=(c == ncc - 1),
                               wait=mts, oap=p[5][0:1, 0:512])
                            att_guard[c % 2] = kb.tok("t")
                        zev = evac("v", bz, AF32[0:1, 4608:5120], src=p[5][0:1, 0:512])
                        kb.op("v", lambda e: e.tensor_single_scalar(
                            AF32[0:1, 4608:5120], AF32[0:1, 4608:5120], EPS_B, ALU.add),
                            wait=[zev], inc=False)
                        kb.op("v", lambda e: e.reciprocal(AF32[0:1, 4608:5120], AF32[0:1, 4608:5120]),
                              inc=False)
                        zbf = kb.op("v", lambda e: e.tensor_copy(SM[0:1, 5632:6144], AF32[0:1, 4608:5120]),
                                    inc=True)
                        bzb = P[2]
                        mm(bzb, c_oner[:], SM[0:1, 5632:6144], start=True, stop=True, wait=[zbf])
                        ze2 = evac("v", bzb, SM[:, 5120:5632])
                        ot_prev = kb.op("v", lambda e, o=ABA[:, OFMB + h * 1024 + tqb * 512:
                                                             OFMB + h * 1024 + (tqb + 1) * 512],
                                        a=bo.ap, b2=SM[:, 5120:5632]: e.tensor_tensor(o, a, b2, ALU.mult),
                                        wait=[ze2, bo.w], inc=True)
                        bo.r = ot_prev
                    lk_tok = kb.tok("t")
                    lv_tok = kb.tok("t")
                kb.barrier()
                for sl in range(4):
                    buf = (sl % 2) * 8192
                    wld = dma("y", WS[:, buf:buf + 8192],
                              b_oT[:, sl * 512:(sl + 1) * 512].rearrange("(c q) n -> q c n", q=128),
                              wait=[kb.tok("t")])
                    for t_ in range(8):
                        bank = P[t_ % 2]
                        for c in range(16):
                            mm(bank, ABA[:, OFMB + c * 1024 + t_ * 128:OFMB + c * 1024 + (t_ + 1) * 128],
                               WS[:, buf + c * 512:buf + (c + 1) * 512],
                               start=(c == 0), stop=(c == 15), wait=[wld] if c == 0 else [])
                        evac("v", bank, ABA[:, QFMB + t_ * 2048 + sl * 512: QFMB + t_ * 2048 + (sl + 1) * 512])
                kb.barrier()
                buf_tok = [None, None]
                for t_ in range(8):
                    xb = AF32[:, (t_ % 2) * 2048:(t_ % 2) * 2048 + 2048]
                    ldx = dma("y", xb, y[row0 + t_ * 128: row0 + (t_ + 1) * 128, :],
                              wait=[buf_tok[t_ % 2]])
                    adt = kb.op("v", lambda e, o=xb, b2=ABA[:, QFMB + t_ * 2048:QFMB + (t_ + 1) * 2048]:
                                e.tensor_add(o, o, b2), wait=[ldx], inc=True)
                    buf_tok[t_ % 2] = dma("g", y[row0 + t_ * 128: row0 + (t_ + 1) * 128, :], xb,
                                          wait=[adt])
        kb.barrier()

    if N_PHASES >= 4:
        ffn_phase()

    sems = {}
    with ExitStack() as sem_st:
        for e in ENGS:
            sems[e] = sem_st.enter_context(nc.semaphore(f"sem_{e}"))
        with nc.Block() as block:
            def mk_body(ename):
                def body(eng):
                    for entry in kb.ops[ename]:
                        if entry[0] == "w":
                            eng.wait_ge(sems[entry[1]], entry[2])
                        else:
                            inst = entry[1](eng)
                            if entry[2] is not None:
                                inst.then_inc(sems[ename], 16 if ename in ("g", "y") else 1)
                return body
            block.tensor(mk_body("t"))
            block.vector(mk_body("v"))
            block.scalar(mk_body("s"))
            block.gpsimd(mk_body("g"))
            block.sync(mk_body("y"))
    st.close()
    return nc


def make_consts():
    c = {}
    c["ident"] = np.eye(128, dtype=np.float32)
    inv = 1.0 / (10000.0 ** (np.arange(0, RD, 2, dtype=np.float32) / RD))
    ang = np.arange(T, dtype=np.float32)[:, None] * inv[None, :]
    c["cos_t"] = np.cos(ang).astype(np.float32)
    c["sin_t"] = np.sin(ang).astype(np.float32)
    r = np.arange(128)
    c["m_su"] = (r[:, None] < r[None, :]).astype(np.float32)
    c["m_li"] = (r[:, None] >= r[None, :]).astype(np.float32)
    c["m_ui"] = (r[:, None] <= r[None, :]).astype(np.float32)
    cv = np.zeros((128, 8), np.float32)
    cv[:, 0] = -0.46875
    cv[:, 1] = 7.5
    cv[:, 2] = -8.0
    cv[:, 3] = -20.0
    cv[:, 4] = EPS_A
    cv[:, 5] = EPS_LN
    c["cvec"] = cv
    return c


_consts = None


def kernel(**inputs):
    global _consts
    if _consts is None:
        _consts = make_consts()
    n_cores = 8
    x = np.ascontiguousarray(inputs["L0_attn_a_h"], dtype=np.float32)
    base = {
        "wqc": np.ascontiguousarray(inputs["L0_attn_a_wq_comp"], dtype=np.float32),
        "wqu": np.ascontiguousarray(inputs["L0_attn_a_wq_up"], dtype=np.float32),
        "wk": np.ascontiguousarray(inputs["L0_attn_a_wk"], dtype=np.float32),
        "wv": np.ascontiguousarray(inputs["L0_attn_a_wv"], dtype=np.float32),
        "w1": np.ascontiguousarray(inputs["L0_attn_a_out_w1"], dtype=np.float32).reshape(2048, 2048),
        "w2": np.ascontiguousarray(inputs["L0_attn_a_out_w2"], dtype=np.float32),
        "wqb": np.ascontiguousarray(inputs["L0_attn_b_Wq"], dtype=np.float32),
        "wkb": np.ascontiguousarray(inputs["L0_attn_b_Wk"], dtype=np.float32),
        "wvb": np.ascontiguousarray(inputs["L0_attn_b_Wv"], dtype=np.float32),
        "wob": np.ascontiguousarray(inputs["L0_attn_b_Wo"], dtype=np.float32),
        "gup_p": np.ascontiguousarray(inputs["L0_ffn_gate_up_packed"], dtype=np.int32),
        "gup_s": np.ascontiguousarray(inputs["L0_ffn_gate_up_scales"].T),
        "dwn_p": np.ascontiguousarray(inputs["L0_ffn_down_packed"], dtype=np.int32),
        "dwn_s": np.ascontiguousarray(inputs["L0_ffn_down_scales"].T),
        "qnw4": np.ascontiguousarray(np.broadcast_to(
            np.tile(np.asarray(inputs["L0_attn_a_q_norm_w"], dtype=np.float32), 4)[None, :], (128, 512))),
        "knw": np.ascontiguousarray(np.broadcast_to(
            np.asarray(inputs["L0_attn_a_k_norm_w"], dtype=np.float32)[None, :], (128, 128))),
        "gam4": np.ascontiguousarray(np.broadcast_to(
            np.tile(np.asarray(inputs["L0_attn_b_gamma"], dtype=np.float32), 4)[None, :], (128, 512))),
        "bet4": np.ascontiguousarray(np.broadcast_to(
            np.tile(np.asarray(inputs["L0_attn_b_beta"], dtype=np.float32), 4)[None, :], (128, 512))),
        "sinkb": np.ascontiguousarray(np.broadcast_to(
            np.asarray(inputs["L0_attn_a_sink_logit"], dtype=np.float32)[None, :], (128, 16))),
    }
    base.update(_consts)
    nc = build_nc()
    in_maps = []
    for c in range(n_cores):
        m = dict(base)
        m["x"] = np.ascontiguousarray(x[2 * c:2 * c + 2].reshape(2 * T, DM))
        in_maps.append(m)
    res = run_bass_kernel_spmd(nc, in_maps, core_ids=list(range(n_cores)))
    out = np.empty((16, T, DM), np.float32)
    for c in range(n_cores):
        out[2 * c:2 * c + 2] = res.results[c]["y"].reshape(2, T, DM)
    return out
